# version 01 (final)
# baseline (speedup 1.0000x reference)
import numpy as np
from contextlib import ExitStack

import concourse.bass as bass
import concourse.mybir as mybir
from concourse.bass_utils import run_bass_kernel_spmd

F32 = mybir.dt.float32
BF16 = mybir.dt.bfloat16
I32 = mybir.dt.int32
AF = mybir.ActivationFunctionType
ALU = mybir.AluOpType

B, T, D = 2, 4096, 2048
NH, HD = 16, 128
HPC = 4
DC = 16
TCH = 8
TCW = 512
CJ = 32
C = 128
HID = 5632
HT = 44
MT = 16
TQ = 1024
EPS_RMS = 1e-5
EPS_L2 = 1e-6
EPS_LN = 1e-5
LOG128_HALF = 0.5 * float(np.log(HD))

ENGINES = ("tensor", "vector", "scalar", "gpsimd", "sync")


class Rec:
    def __init__(self):
        self.items = {e: [] for e in ENGINES}
        self.last = {e: None for e in ENGINES}
        self.sem = {}
        self.cnt = {}
        self.pending_waits = {e: {} for e in ENGINES}

    def init_sems(self, nc, ctx):
        self.nc = nc
        self.ctx = ctx
        for name in ["pe", "dve", "act", "pool", "cc"]:
            self.sem[name] = ctx.enter_context(nc.semaphore("sem_" + name))
            self.cnt[name] = 0

    def _slot_sem(self, slot):
        name = "q_" + slot
        if name not in self.sem:
            self.sem[name] = self.ctx.enter_context(self.nc.semaphore(name))
            self.cnt[name] = 0
        return name

    def add(self, eng, fn, is_dma=False, slot=None):
        pw = self.pending_waits[eng]
        if pw:
            for sname, val in pw.items():
                self.items[eng].append(
                    {"fn": (lambda e, s=self.sem[sname], v=val: e.wait_ge(s, v)),
                     "incs": []})
            self.pending_waits[eng] = {}
        item = {"fn": fn, "incs": [], "is_dma": is_dma, "eng": eng, "slot": slot}
        self.items[eng].append(item)
        self.last[eng] = (item, is_dma, eng)
        if is_dma:
            assert slot is not None, "DMA needs a slot name"
            self.tok(eng)
        return item

    def tok(self, eng):
        item, is_dma, _ = self.last[eng]
        if "tok0" in item:
            return item["tok0"]
        if is_dma:
            sname = self._slot_sem(item["slot"])
            k = 16
        else:
            sname = {"tensor": "pe", "vector": "dve", "scalar": "act",
                     "gpsimd": "pool"}[eng]
            k = 1
        self.cnt[sname] += k
        item["incs"].append((self.sem[sname], k))
        item["tok0"] = (sname, self.cnt[sname])
        return item["tok0"]

    def wait(self, eng, *toks):
        pw = self.pending_waits[eng]
        own = {"tensor": {"pe"}, "vector": {"dve"}, "scalar": {"act"},
               "gpsimd": {"pool"}, "sync": set()}[eng]
        for t in toks:
            if t is None:
                continue
            sname, val = t
            if sname in own:
                continue
            if pw.get(sname, 0) < val:
                pw[sname] = val

    def barrier(self):
        for e in ("tensor", "vector", "scalar", "gpsimd"):
            if self.last[e] is not None:
                self.tok(e)
        toks = [(n, c) for n, c in self.cnt.items() if c > 0]
        for e in ENGINES:
            self.wait(e, *toks)

    def replay(self, block):
        for eng in ENGINES:
            items = self.items[eng]
            if not items:
                continue

            def body(e, items=items):
                for it in items:
                    inst = it["fn"](e)
                    for (s, k) in it["incs"]:
                        inst = inst.then_inc(s, k)
            getattr(block, eng)(body)


def final_wait(R, eng, tok):
    sname, val = tok
    R.items[eng].append(
        {"fn": (lambda e, s=R.sem[sname], v=val: e.wait_ge(s, v)), "incs": []})


def build_program(debug_taps=()):
    nc = bass.Bass("TRN2", target_bir_lowering=False, debug=False)
    nc.has_collectives = True

    xT = nc.dram_tensor("xT", [128, DC, T], F32, kind="ExternalInput")
    xTs = nc.dram_tensor("xTs", [D, TQ], F32, kind="ExternalInput")
    wqkvg = nc.dram_tensor("wqkvg", [4, 128, DC, 512], F32, kind="ExternalInput")
    wb = nc.dram_tensor("wb", [128, DC, HPC], F32, kind="ExternalInput")
    cw = nc.dram_tensor("cw", [128, 48], F32, kind="ExternalInput")
    normw = nc.dram_tensor("normw", [128, 1], F32, kind="ExternalInput")
    wo = nc.dram_tensor("wo", [128, DC, D], F32, kind="ExternalInput")
    wqgT = nc.dram_tensor("wqgT", [HT, 128, DC, 128], I32, kind="ExternalInput")
    wquT = nc.dram_tensor("wquT", [HT, 128, DC, 128], I32, kind="ExternalInput")
    wqdT = nc.dram_tensor("wqdT", [MT, 128, HT, 128], I32, kind="ExternalInput")
    sgv = nc.dram_tensor("sgv", [128, HT], F32, kind="ExternalInput")
    suv = nc.dram_tensor("suv", [128, HT], F32, kind="ExternalInput")
    sdv = nc.dram_tensor("sdv", [128, MT], F32, kind="ExternalInput")
    selmask = nc.dram_tensor("selmask", [128, 8], F32, kind="ExternalInput")
    outT = nc.dram_tensor("outT", [D, TQ], F32, kind="ExternalOutput")

    str_t = nc.dram_tensor("str_t", [CJ, HPC, 128, 4 * C], BF16, kind="Internal")
    wo_bf = nc.dram_tensor("wo_bf", [MT, 128, DC, 128], BF16, kind="Internal")
    og_in = nc.dram_tensor("og_in", [512, T], BF16, kind="Internal")
    og_out = nc.dram_tensor("og_out", [8 * 512, T], BF16, kind="Internal",
                            addr_space="Shared")
    taps = {}
    if "str_t" in debug_taps:
        taps["str_t"] = nc.dram_tensor("dbg_str", [CJ, HPC, 128, 4 * C], BF16,
                                       kind="ExternalOutput")
    if "og_in" in debug_taps:
        taps["og_in"] = nc.dram_tensor("dbg_og", [512, T], BF16,
                                       kind="ExternalOutput")
    if "scan0" in debug_taps:
        taps["scan0"] = nc.dram_tensor("dbg_scan0", [6, 128, 128], BF16,
                                       kind="ExternalOutput")
    if "c1" in debug_taps:
        taps["c1"] = nc.dram_tensor("dbg_c1", [3, 128, TQ], F32,
                                    kind="ExternalOutput")
        taps["c1b"] = nc.dram_tensor("dbg_c1b", [2, 128, TQ], BF16,
                                     kind="ExternalOutput")
    if "c3" in debug_taps:
        taps["c3"] = nc.dram_tensor("dbg_c3", [8, 128, TQ], BF16,
                                    kind="ExternalOutput")
    if "zfull" in debug_taps:
        taps["zfull"] = nc.dram_tensor("dbg_zf", [MT, 128, TQ], F32,
                                       kind="ExternalOutput")

    R = Rec()
    ctx = ExitStack()
    with ctx:
        R.init_sems(nc, ctx)

        ident = ctx.enter_context(nc.sbuf_tensor("ident", [128, 128], BF16))
        m_trs = ctx.enter_context(nc.sbuf_tensor("m_trs", [128, 128], BF16))
        m_tri = ctx.enter_context(nc.sbuf_tensor("m_tri", [128, 128], BF16))
        ones_c = ctx.enter_context(nc.sbuf_tensor("ones_c", [128, 1], BF16))
        ones_cf = ctx.enter_context(nc.sbuf_tensor("ones_cf", [128, 1], F32))
        ones_r = ctx.enter_context(nc.sbuf_tensor("ones_r", [1, 128], BF16))
        neg128 = ctx.enter_context(nc.sbuf_tensor("neg128", [1, 128], BF16))
        cw_sb = ctx.enter_context(nc.sbuf_tensor("cw_sb", [128, 48], F32))
        nw_sb = ctx.enter_context(nc.sbuf_tensor("nw_sb", [128, 1], F32))
        sg_sb = ctx.enter_context(nc.sbuf_tensor("sg_sb", [128, HT], F32))
        su_sb = ctx.enter_context(nc.sbuf_tensor("su_sb", [128, HT], F32))
        sd_sb = ctx.enter_context(nc.sbuf_tensor("sd_sb", [128, MT], F32))
        sel_sb = ctx.enter_context(nc.sbuf_tensor("sel_sb", [128, 8], F32))
        cvals = ctx.enter_context(nc.sbuf_tensor("cvals", [128, 8], F32))
        phAB = ExitStack()
        beta4 = phAB.enter_context(nc.sbuf_tensor("beta4", [4, T], BF16))
        S_f32 = phAB.enter_context(nc.sbuf_tensor("S_f32", [128, HPC * 128], F32))
        S_bf = phAB.enter_context(nc.sbuf_tensor("S_bf", [128, HPC * 128], BF16))

        R.add("gpsimd", lambda e: e.memset(ident[:], 1.0))
        R.add("gpsimd", lambda e: e.affine_select(
            out=ident[:], in_=ident[:], compare_op=ALU.is_equal, fill=0.0,
            base=0, pattern=[[-1, 128]], channel_multiplier=1))
        R.add("gpsimd", lambda e: e.memset(m_trs[:], 1.0))
        R.add("gpsimd", lambda e: e.affine_select(
            out=m_trs[:], in_=m_trs[:], compare_op=ALU.is_gt, fill=0.0,
            base=0, pattern=[[1, 128]], channel_multiplier=-1))
        R.add("gpsimd", lambda e: e.memset(m_tri[:], 1.0))
        R.add("gpsimd", lambda e: e.affine_select(
            out=m_tri[:], in_=m_tri[:], compare_op=ALU.is_ge, fill=0.0,
            base=0, pattern=[[1, 128]], channel_multiplier=-1))
        R.add("gpsimd", lambda e: e.memset(ones_c[:], 1.0))
        R.add("gpsimd", lambda e: e.memset(ones_cf[:], 1.0))
        R.add("gpsimd", lambda e: e.memset(ones_r[:], 1.0))
        R.add("gpsimd", lambda e: e.memset(neg128[:], -128.0))
        for i, v in enumerate([EPS_L2, -LOG128_HALF, EPS_RMS, -0.5,
                               1.0 / HD, 1.0 / D, 0.0, 1.0]):
            R.add("gpsimd", lambda e, i=i, v=v: e.memset(cvals[:, i:i + 1], v))
            nc.const_aps.register(F32, v, cvals[:, i:i + 1])
        for (dst, src) in [(cw_sb, cw), (nw_sb, normw), (sg_sb, sgv),
                           (su_sb, suv), (sd_sb, sdv), (sel_sb, selmask)]:
            R.add("sync", lambda e, d=dst, s=src: e.dma_start(out=d[:], in_=s.ap()),
                  is_dma=True, slot="cst")
        R.add("vector", lambda e: e.memset(S_f32[:], 0.0))
        R.add("vector", lambda e: e.memset(S_bf[:], 0.0))
        R.barrier()

        phA = ExitStack()
        with phA:
            wst = phA.enter_context(nc.sbuf_tensor("wst", [128, DC * 256], F32))
            wbf = {w: phA.enter_context(nc.sbuf_tensor(f"wbf_{w}", [128, DC * 512], BF16))
                   for w in ("q", "k", "v", "g")}
            wbst = phA.enter_context(nc.sbuf_tensor("wbst", [128, DC * HPC], F32))
            wbbf = phA.enter_context(nc.sbuf_tensor("wbbf", [128, DC * HPC], BF16))
            xf = [phA.enter_context(nc.sbuf_tensor(f"xf{i}", [128, 8 * TCW], F32))
                  for i in range(2)]
            xb = phA.enter_context(nc.sbuf_tensor("xb", [128, DC * TCW], BF16))
            raw = {w: phA.enter_context(nc.sbuf_tensor(f"raw_{w}", [128, 4 * 515], BF16))
                   for w in ("q", "k", "v")}
            acc = {w: phA.enter_context(nc.sbuf_tensor(f"acc_{w}", [128, 4 * TCW], F32))
                   for w in ("q", "k", "v")}
            post = {w: phA.enter_context(nc.sbuf_tensor(f"post_{w}", [128, 4 * TCW], BF16))
                    for w in ("q", "k", "v", "g")}
            sqt = phA.enter_context(nc.sbuf_tensor("sqt", [128, 4 * TCW], BF16))
            rstd = phA.enter_context(nc.sbuf_tensor("rstd", [1, 4 * TCW], BF16))
            ps_pj = [phA.enter_context(nc.psum_tensor(f"ps_pj{i}", [128, TCW], F32))
                     for i in range(4)]
            ps_b = phA.enter_context(nc.psum_tensor("ps_b", [HPC, TCW], F32))
            ps_ss = phA.enter_context(nc.psum_tensor("ps_ss", [1, TCW], F32))
            ps_bc = phA.enter_context(nc.psum_tensor("ps_bc", [128, TCW], F32))

            cast3 = ["vector", "scalar", "gpsimd"]

            def copy_op(eng, dst, src):
                if eng == "scalar":
                    R.add("scalar", lambda e, d=dst, s=src: e.copy(out=d, in_=s))
                else:
                    R.add(eng, lambda e, d=dst, s=src: e.tensor_copy(out=d, in_=s))

            t_wcast = []
            t_wst_users = None
            for wi, w in enumerate(("q", "k", "v", "g")):
                for hf in range(2):
                    if t_wst_users:
                        R.wait("sync", *t_wst_users)
                    R.add("sync", lambda e, wi=wi, hf=hf: e.dma_start(
                        out=wst[:],
                        in_=wqkvg.ap()[wi].rearrange("p c j -> p (c j)")
                        [:, hf * 4096:(hf + 1) * 4096]), is_dma=True, slot="wst")
                    t_ld = R.tok("sync")
                    t_wst_users = []
                    for sl in range(2):
                        eng = cast3[(wi * 4 + hf * 2 + sl) % 3]
                        R.wait(eng, t_ld)
                        copy_op(eng, wbf[w][:, (hf * 2 + sl) * 2048:(hf * 2 + sl + 1) * 2048],
                                wst[:, sl * 2048:(sl + 1) * 2048])
                        tk = R.tok(eng)
                        t_wst_users.append(tk)
                        t_wcast.append(tk)
            R.wait("sync", *t_wst_users)
            R.add("sync", lambda e: e.dma_start(
                out=wbst[:],
                in_=wb.ap().rearrange("p c j -> p (c j)")), is_dma=True, slot="wst")
            t_ldb = R.tok("sync")
            R.wait("vector", t_ldb)
            copy_op("vector", wbbf[:], wbst[:])
            t_wcast.append(R.tok("vector"))

            for w in ("q", "k", "v"):
                R.add("vector", lambda e, r=raw[w]: e.memset(r[:], 0.0))
            t_rawz = R.tok("vector")

            slot = {"k": 0, "q": 1, "v": 2, "g": 3}
            evac_eng = {"q": "scalar", "k": "vector", "v": "scalar", "g": "vector"}
            conv_eng = {"q": "vector", "k": "vector", "v": "vector"}

            t_xcast = {}
            t_xb_users = None
            t_conv = {}
            t_silu_prev = {}
            t_str = {}
            t_sqt_users = None
            t_rstd_users = None
            t_ps_ss_user = None
            t_ps_bc_user = None
            for tc in range(TCH):
                t_ldx = []
                for ch in range(2):
                    if tc >= 1:
                        R.wait("sync", *t_xcast[(tc - 1, ch)])
                    R.add("sync", lambda e, t0=tc * TCW, ch=ch: e.dma_start(
                        out=xf[ch].ap().rearrange("p (c t) -> p c t", t=TCW),
                        in_=xT.ap()[:, ch * 8:(ch + 1) * 8, t0:t0 + TCW]), is_dma=True, slot="xf%d" % ch)
                    t_ldx.append(R.tok("sync"))
                t_xcast[(tc, 0)] = []
                t_xcast[(tc, 1)] = []
                for dc in range(DC):
                    ch = dc // 8
                    eng = cast3[dc % 3]
                    R.wait(eng, t_ldx[ch])
                    if t_xb_users:
                        R.wait(eng, *t_xb_users)
                    copy_op(eng, xb[:, dc * TCW:(dc + 1) * TCW],
                            xf[ch][:, (dc % 8) * TCW:(dc % 8 + 1) * TCW])
                    t_xcast[(tc, ch)].append(R.tok(eng))
                t_xc_all = t_xcast[(tc, 0)] + t_xcast[(tc, 1)]

                t_ev = {}
                t_psfree = [None, None, None, None]
                gidx = 0
                for w in ("q", "k", "v", "g"):
                    for j in range(4):
                        pst = ps_pj[gidx % 4]
                        R.wait("tensor", *t_xc_all)
                        if tc == 0 and w == "q" and j == 0:
                            R.wait("tensor", *t_wcast)
                        if t_psfree[gidx % 4]:
                            R.wait("tensor", t_psfree[gidx % 4])
                        for kc in range(DC):
                            R.add("tensor", lambda e, pst=pst, w=w, kc=kc, j=j:
                                  e.matmul(out=pst[:],
                                           lhsT=wbf[w][:, kc * 512 + j * 128:kc * 512 + (j + 1) * 128],
                                           rhs=xb[:, kc * TCW:(kc + 1) * TCW],
                                           start=(kc == 0), stop=(kc == DC - 1)))
                        t_mm = R.tok("tensor")
                        eng = evac_eng[w]
                        R.wait(eng, t_mm)
                        if w == "g":
                            if tc > 0:
                                R.wait(eng, t_str[tc - 1])
                            copy_op(eng, post["g"][:, j * TCW:(j + 1) * TCW], pst[:])
                        else:
                            if tc > 0:
                                R.wait(eng, t_conv[(w, tc - 1)])
                            else:
                                R.wait(eng, t_rawz)
                            copy_op(eng, raw[w][:, j * 515:j * 515 + 3],
                                    raw[w][:, j * 515 + 512:j * 515 + 515])
                            copy_op(eng, raw[w][:, j * 515 + 3:(j + 1) * 515], pst[:])
                        t_ev[(w, j)] = t_psfree[gidx % 4] = R.tok(eng)
                        gidx += 1
                if t_ps_ss_user:
                    pass
                R.wait("tensor", *t_xc_all)
                if tc > 0:
                    R.wait("tensor", t_beta_ev)
                for kc in range(DC):
                    R.add("tensor", lambda e, kc=kc: e.matmul(
                        out=ps_b[:], lhsT=wbbf[:, kc * HPC:(kc + 1) * HPC],
                        rhs=xb[:, kc * TCW:(kc + 1) * TCW],
                        start=(kc == 0), stop=(kc == DC - 1)))
                t_mmb = R.tok("tensor")
                t_xb_users = [t_mmb]
                R.wait("scalar", t_mmb)
                R.add("scalar", lambda e, t0=tc * TCW: e.activation(
                    out=beta4[:, t0:t0 + TCW], in_=ps_b[:], func=AF.Sigmoid))
                t_beta_ev = R.tok("scalar")

                for w in ("q", "k", "v"):
                    eng = conv_eng[w]
                    R.wait(eng, *[t_ev[(w, j)] for j in range(4)])
                    if t_silu_prev.get(w):
                        R.wait(eng, t_silu_prev[w])
                    woff = {"q": 0, "k": 16, "v": 32}[w]
                    for dc in range(4):
                        for tap in range(4):
                            src = raw[w][:, dc * 515 + tap:dc * 515 + tap + 512]
                            dsta = acc[w][:, dc * TCW:(dc + 1) * TCW]
                            scl = cw_sb[:, woff + dc * 4 + tap:woff + dc * 4 + tap + 1]
                            if tap == 0:
                                R.add(eng, lambda e, d=dsta, s=src, c=scl: e.tensor_scalar(
                                    out=d, in0=s, scalar1=c, scalar2=None, op0=ALU.mult))
                            else:
                                R.add(eng, lambda e, d=dsta, s=src, c=scl:
                                      e.scalar_tensor_tensor(out=d, in0=s, scalar=c,
                                                             in1=d, op0=ALU.mult, op1=ALU.add))
                    t_conv[(w, tc)] = R.tok(eng)

                R.wait("scalar", t_conv[("v", tc)])
                if tc > 0:
                    R.wait("scalar", t_str[tc - 1])
                R.add("scalar", lambda e: e.activation(
                    out=post["v"][:], in_=acc["v"][:], func=AF.Silu))
                t_pv = R.tok("scalar")
                t_silu_prev["v"] = t_pv

                t_qk_norm = []
                for w in ("q", "k"):
                    R.wait("scalar", t_conv[(w, tc)])
                    if t_sqt_users:
                        R.wait("scalar", *t_sqt_users)
                    R.add("scalar", lambda e, w=w: e.activation(
                        out=sqt[:], in_=acc[w][:], func=AF.Silu))
                    t_silu = R.tok("scalar")
                    t_silu_prev[w] = t_silu
                    R.wait("vector", t_silu)
                    if tc > 0 and w == "q":
                        R.wait("vector", t_str[tc - 1])
                    copy_op("vector", post[w][:], sqt[:])
                    R.add("vector", lambda e: e.tensor_mul(out=sqt[:], in0=sqt[:], in1=sqt[:]))
                    t_sq = R.tok("vector")
                    t_sqt_users = []
                    for h in range(4):
                        R.wait("tensor", t_sq)
                        if t_ps_ss_user:
                            R.wait("tensor", t_ps_ss_user)
                        R.add("tensor", lambda e, h=h: e.matmul(
                            out=ps_ss[:], lhsT=ones_c[:],
                            rhs=sqt[:, h * TCW:(h + 1) * TCW], start=True, stop=True))
                        t_ssmm = R.tok("tensor")
                        t_sqt_users.append(t_ssmm)
                        R.wait("scalar", t_ssmm)
                        if t_rstd_users:
                            R.wait("scalar", *t_rstd_users)
                            t_rstd_users = None
                        R.add("scalar", lambda e, h=h: e.activation(
                            out=rstd[:, h * TCW:(h + 1) * TCW], in_=ps_ss[:],
                            func=AF.Ln, bias=EPS_L2))
                        t_ps_ss_user = R.tok("scalar")
                        bias = -LOG128_HALF if w == "q" else 0.0
                        R.add("scalar", lambda e, h=h, b=bias: e.activation(
                            out=rstd[:, h * TCW:(h + 1) * TCW],
                            in_=rstd[:, h * TCW:(h + 1) * TCW],
                            func=AF.Exp, scale=-0.5, bias=b))
                        t_rs = R.tok("scalar")
                        R.wait("tensor", t_rs)
                        if t_ps_bc_user:
                            R.wait("tensor", t_ps_bc_user)
                        R.add("tensor", lambda e, h=h: e.matmul(
                            out=ps_bc[:], lhsT=ones_r[:],
                            rhs=rstd[:, h * TCW:(h + 1) * TCW], start=True, stop=True))
                        t_bc = R.tok("tensor")
                        t_rstd_users = [t_bc]
                        R.wait("vector", t_bc)
                        R.add("vector", lambda e, w=w, h=h: e.tensor_mul(
                            out=post[w][:, h * TCW:(h + 1) * TCW],
                            in0=post[w][:, h * TCW:(h + 1) * TCW], in1=ps_bc[:]))
                        t_nm = R.tok("vector")
                        t_ps_bc_user = t_nm
                        t_qk_norm.append(t_nm)

                R.wait("gpsimd", t_pv, *t_qk_norm)
                t_wr = []
                for w in ("q", "k", "v", "g"):
                    for jj in range(4):
                        R.add("gpsimd", lambda e, w=w, jj=jj, tc=tc: e.dma_start(
                            out=str_t.ap()[4 * tc:4 * tc + 4, :, :,
                                           slot[w] * C:(slot[w] + 1) * C]
                            .rearrange("jj h p t -> p h jj t")[:, :, jj],
                            in_=post[w].ap().rearrange("p (h jj t) -> p h jj t",
                                                       h=4, jj=4, t=C)[:, :, jj]),
                            is_dma=True, slot="strw")
                        t_wr.append(R.tok("gpsimd"))
                t_str[tc] = t_wr[-1]

        if "str_t" in taps:
            R.wait("gpsimd", t_str[TCH - 1])
            R.add("gpsimd", lambda e: e.dma_start(out=taps["str_t"].ap(),
                                                  in_=str_t.ap()), is_dma=True, slot="tap")
        R.barrier()

        phB = ExitStack()
        with phB:
            stt = [phB.enter_context(nc.sbuf_tensor(f"stt{i}", [128, HPC * 4 * C], BF16))
                   for i in range(2)]
            bcolT = phB.enter_context(nc.sbuf_tensor("bcolT", [128, 4], F32))
            U1 = phB.enter_context(nc.sbuf_tensor("U1", [128, 128], BF16))
            Lt = phB.enter_context(nc.sbuf_tensor("Lt", [128, 128], BF16))
            N1 = phB.enter_context(nc.sbuf_tensor("N1", [128, 128], BF16))
            POW = {p: phB.enter_context(nc.sbuf_tensor(f"POW{p}", [128, 128], BF16))
                   for p in (2, 4, 8, 16, 32, 64)}
            NPW = {p: phB.enter_context(nc.sbuf_tensor(f"NPW{p}", [128, 128], BF16))
                   for p in (2, 4, 8, 16, 32)}
            Vtm = phB.enter_context(nc.sbuf_tensor("Vtm", [128, 128], BF16))
            Ktm = phB.enter_context(nc.sbuf_tensor("Ktm", [128, 128], BF16))
            Zb = phB.enter_context(nc.sbuf_tensor("Zb", [128, 128], BF16))
            Wt = phB.enter_context(nc.sbuf_tensor("Wt", [128, 128], BF16))
            sqo = phB.enter_context(nc.sbuf_tensor("sqo", [128, 128], BF16))
            rso = phB.enter_context(nc.sbuf_tensor("rso", [1, 128], BF16))
            sgo = phB.enter_context(nc.sbuf_tensor("sgo", [128, 128], BF16))
            tgo = phB.enter_context(nc.sbuf_tensor("tgo", [128, 128], BF16))
            wost = phB.enter_context(nc.sbuf_tensor("wost", [128, DC * 256], F32))
            wqbf = phB.enter_context(nc.sbuf_tensor("wqbf", [128, DC * 256], BF16))
            og_sb = phB.enter_context(nc.sbuf_tensor("og_sb", [128, HPC * T], BF16))

            ps_al = phB.enter_context(nc.psum_tensor("ps_al", [128, 256], F32))
            ps_tr = phB.enter_context(nc.psum_tensor("ps_tr", [128, 128], BF16))
            ps_pw = [phB.enter_context(nc.psum_tensor(f"ps_pw{i}", [128, 128], F32))
                     for i in range(2)]
            ps_y = phB.enter_context(nc.psum_tensor("ps_y", [128, 128], F32))
            ps_o = phB.enter_context(nc.psum_tensor("ps_o", [128, 128], F32))
            ps_misc = phB.enter_context(nc.psum_tensor("ps_misc", [128, 256], F32))
            ps_ds = ps_misc[:, 0:128]
            ps_s1 = ps_misc[0:1, 128:256]
            ps_b2 = phB.enter_context(nc.psum_tensor("ps_b2", [128, 128], F32))

            t_wost_users = None
            t_wqbf_st = None
            for qtr in range(8):
                if t_wost_users:
                    R.wait("sync", *t_wost_users)
                for chh in range(2):
                    R.add("sync", lambda e, q=qtr, ch=chh: e.dma_start(
                        out=wost.ap().rearrange("p (c t) -> p c t", t=256)
                        [:, ch * 8:(ch + 1) * 8],
                        in_=wo.ap()[:, ch * 8:(ch + 1) * 8, q * 256:(q + 1) * 256]),
                        is_dma=True, slot="wost")
                t_lwo = R.tok("sync")
                t_wost_users = []
                if t_wqbf_st:
                    R.wait(cast3[0], t_wqbf_st)
                for dct in range(4):
                    eng = cast3[dct % 3]
                    R.wait(eng, t_lwo)
                    if t_wqbf_st:
                        R.wait(eng, t_wqbf_st)
                    for dc in range(dct * 4, (dct + 1) * 4):
                        copy_op(eng, wqbf[:, dc * 256:(dc + 1) * 256],
                                wost[:, dc * 256:(dc + 1) * 256])
                    tk = R.tok(eng)
                    t_wost_users.append(tk)
                R.wait("gpsimd", *t_wost_users)
                for mh in range(2):
                    for chh in range(2):
                        R.add("gpsimd", lambda e, q=qtr, mh=mh, ch=chh: e.dma_start(
                            out=wo_bf.ap()[2 * q + mh][:, ch * 8:(ch + 1) * 8],
                            in_=wqbf.ap().rearrange("p (c t) -> p c t", t=256)
                            [:, ch * 8:(ch + 1) * 8, mh * 128:(mh + 1) * 128]),
                            is_dma=True, slot="wobw")
                t_wqbf_st = R.tok("gpsimd")

            def stv(par, h, sl):
                return stt[par][:, (h * 4 + sl) * C:(h * 4 + sl + 1) * C]

            t_ld_st = {}
            t_chunk_users = {}
            t_sbf = {h: None for h in range(HPC)}
            t_pe_head_prev = None
            t_tgo_prev = None
            t_bcol_user = None
            ps_tokens = {k: None for k in
                         ["al", "tr", "pw0", "pw1", "y", "o", "ds", "s1", "b2", "bt"]}
            for j in range(CJ):
                par = j % 2
                if j >= 2:
                    R.wait("sync", *t_chunk_users[j - 2])
                for h in range(HPC):
                    R.add("sync", lambda e, j=j, h=h, par=par: e.dma_start(
                        out=stt[par][:, h * 4 * C:(h + 1) * 4 * C],
                        in_=str_t.ap()[j, h]), is_dma=True, slot="st%d%d" % (par, h))
                    t_ld_st[(j, h)] = R.tok("sync")
                t_chunk_users[j] = []

                if ps_tokens["tr"]:
                    R.wait("tensor", ps_tokens["tr"])
                R.add("tensor", lambda e, j=j: e.transpose(
                    out=ps_tr[:, 0:4], in_=beta4[:, j * C:(j + 1) * C],
                    identity=ident[0:4, 0:4]))
                t_btmm = R.tok("tensor")
                R.wait("scalar", t_btmm)
                if t_bcol_user:
                    R.wait("scalar", *t_bcol_user)
                R.add("scalar", lambda e: e.copy(out=bcolT[:], in_=ps_tr[:, 0:4]))
                t_bcol = ps_tokens["tr"] = R.tok("scalar")
                t_bcol_user = []

                for h in range(HPC):
                    K_ = stv(par, h, 0)
                    Q_ = stv(par, h, 1)
                    V_ = stv(par, h, 2)
                    G_ = stv(par, h, 3)
                    KQ = stt[par][:, h * 4 * C:h * 4 * C + 2 * C]
                    bcol = bcolT[:, h:h + 1]
                    Sf = S_f32[:, h * 128:(h + 1) * 128]
                    Sb = S_bf[:, h * 128:(h + 1) * 128]

                    R.wait("tensor", t_ld_st[(j, h)])
                    if ps_tokens["al"]:
                        R.wait("tensor", *ps_tokens["al"])
                    R.add("tensor", lambda e, K_=K_: e.matmul(
                        out=ps_al[:, 0:128], lhsT=K_, rhs=K_, start=True, stop=True))
                    R.add("tensor", lambda e, K_=K_, Q_=Q_: e.matmul(
                        out=ps_al[:, 128:256], lhsT=K_, rhs=Q_, start=True, stop=True))
                    t_al = R.tok("tensor")
                    t_chunk_users[j].append(t_al)
                    R.wait("vector", t_al, t_bcol, t_pe_head_prev)
                    R.add("vector", lambda e, b=bcol: e.scalar_tensor_tensor(
                        out=U1[:], in0=ps_al[:, 0:128], scalar=b, in1=m_trs[:],
                        op0=ALU.mult, op1=ALU.mult))
                    t_u1 = R.tok("vector")
                    t_bcol_user.append(t_u1)
                    if "scan0" in taps and j == 0 and h == 0:
                        R.wait("scalar", t_al)
                        R.add("scalar", lambda e: e.copy(out=Vtm[:], in_=ps_al[:, 0:128]))
                        t_tapc = R.tok("scalar")
                        R.wait("gpsimd", t_tapc, t_u1, t_bcol)
                        for di, buf in enumerate([U1[:], bcolT[:], Vtm[:],
                                                  beta4[0:4, 0:128]]):
                            R.add("gpsimd", lambda e, di=di, buf=buf: e.dma_start(
                                out=(taps["scan0"].ap()[di] if di != 1 and di != 3
                                     else taps["scan0"].ap()[di][0:buf.shape[0], 0:buf.shape[1]]),
                                in_=buf), is_dma=True, slot="tap")
                        t_tap0 = R.tok("gpsimd")
                        R.wait("vector", t_tap0)
                        R.wait("scalar", t_tap0)
                        R.wait("tensor", t_tap0)
                    R.add("vector", lambda e: e.tensor_mul(
                        out=Lt[:], in0=ps_al[:, 128:256], in1=m_tri[:]))
                    t_lt = R.tok("vector")
                    ps_tokens["al"] = [t_u1, t_lt]

                    if ps_tokens["y"]:
                        R.wait("tensor", ps_tokens["y"])
                    if t_sbf[h]:
                        R.wait("tensor", t_sbf[h])
                    R.add("tensor", lambda e, K_=K_, Sb=Sb: e.matmul(
                        out=ps_y[:], lhsT=K_, rhs=Sb, start=True, stop=True))
                    t_ymm = R.tok("tensor")

                    if ps_tokens["tr"]:
                        R.wait("tensor", ps_tokens["tr"])
                    R.add("tensor", lambda e, V_=V_: e.transpose(
                        out=ps_tr[:], in_=V_, identity=ident[:]))
                    t_vt = R.tok("tensor")
                    R.wait("scalar", t_vt, t_pe_head_prev, t_tgo_prev)
                    R.add("scalar", lambda e: e.copy(out=Vtm[:], in_=ps_tr[:]))
                    t_vtm = ps_tokens["tr"] = R.tok("scalar")
                    R.wait("vector", t_vtm, t_ymm)
                    R.add("vector", lambda e: e.tensor_sub(out=Zb[:], in0=Vtm[:], in1=ps_y[:]))
                    t_z = ps_tokens["y"] = R.tok("vector")

                    R.wait("tensor", t_u1, ps_tokens["tr"])
                    R.add("tensor", lambda e: e.transpose(
                        out=ps_tr[:], in_=U1[:], identity=ident[:]))
                    t_n1mm = R.tok("tensor")
                    R.wait("scalar", t_n1mm)
                    R.add("scalar", lambda e: e.copy(out=N1[:], in_=ps_tr[:]))
                    t_n1 = ps_tokens["tr"] = R.tok("scalar")

                    chain = [("U", 2, "N", 1, "U", 1), ("N", 2, "U", 1, "N", 1),
                             ("U", 4, "N", 2, "U", 2), ("N", 4, "U", 2, "N", 2),
                             ("U", 8, "N", 4, "U", 4), ("N", 8, "U", 4, "N", 4),
                             ("U", 16, "N", 8, "U", 8), ("N", 16, "U", 8, "N", 8),
                             ("U", 32, "N", 16, "U", 16), ("N", 32, "U", 16, "N", 16),
                             ("U", 64, "N", 32, "U", 32)]

                    def pw(kind, p):
                        if kind == "U":
                            return U1 if p == 1 else POW[p]
                        return N1 if p == 1 else NPW[p]

                    ev_rr = ["scalar", "vector", "gpsimd"]
                    pw_done = {("U", 1): t_u1, ("N", 1): t_n1}
                    for ci, (ok, op, lk, lp, rk, rp) in enumerate(chain):
                        key = "pw%d" % (ci % 2)
                        pst = ps_pw[ci % 2]
                        R.wait("tensor", pw_done[(lk, lp)], pw_done[(rk, rp)])
                        if ps_tokens[key]:
                            R.wait("tensor", ps_tokens[key])
                        R.add("tensor", lambda e, pst=pst, l=pw(lk, lp), r=pw(rk, rp):
                              e.matmul(out=pst[:], lhsT=l[:], rhs=r[:],
                                       start=True, stop=True))
                        t_pm = R.tok("tensor")
                        eng = ev_rr[ci % 3]
                        R.wait(eng, t_pm)
                        dst = pw(ok, op)
                        if eng == "gpsimd":
                            eng = "vector"
                            R.wait(eng, t_pm)
                        copy_op(eng, dst[:], pst[:])
                        pw_done[(ok, op)] = ps_tokens[key] = R.tok(eng)

                    for ai, p in enumerate([64, 32, 16, 8, 4, 2, 1]):
                        key = "pw%d" % (ai % 2)
                        pst = ps_pw[ai % 2]
                        R.wait("tensor", pw_done[("U", p)], t_z)
                        if ps_tokens[key]:
                            R.wait("tensor", ps_tokens[key])
                        R.add("tensor", lambda e, pst=pst, l=pw("U", p): e.matmul(
                            out=pst[:], lhsT=l[:], rhs=Zb[:], start=True, stop=True))
                        t_am = R.tok("tensor")
                        R.wait("vector", t_am)
                        if p > 1:
                            R.add("vector", lambda e, pst=pst: e.tensor_add(
                                out=Zb[:], in0=Zb[:], in1=pst[:]))
                        else:
                            R.add("vector", lambda e, pst=pst: e.tensor_sub(
                                out=Zb[:], in0=Zb[:], in1=pst[:]))
                        t_z = ps_tokens[key] = R.tok("vector")
                    R.add("vector", lambda e, b=bcol: e.tensor_scalar(
                        out=Wt[:], in0=Zb[:], scalar1=b, scalar2=None, op0=ALU.mult))
                    t_w = R.tok("vector")
                    t_bcol_user.append(t_w)

                    R.wait("tensor", ps_tokens["tr"])
                    R.add("tensor", lambda e, K_=K_: e.transpose(
                        out=ps_tr[:], in_=K_, identity=ident[:]))
                    t_ktmm = R.tok("tensor")
                    t_chunk_users[j].append(t_ktmm)
                    R.wait("scalar", t_ktmm)
                    R.add("scalar", lambda e: e.copy(out=Ktm[:], in_=ps_tr[:]))
                    t_ktm = ps_tokens["tr"] = R.tok("scalar")

                    R.wait("tensor", t_w, t_lt)
                    if ps_tokens["o"]:
                        R.wait("tensor", *ps_tokens["o"])
                    R.add("tensor", lambda e, Sb=Sb, Q_=Q_: e.matmul(
                        out=ps_o[:], lhsT=Sb, rhs=Q_, start=True, stop=False))
                    R.add("tensor", lambda e: e.matmul(
                        out=ps_o[:], lhsT=Wt[:], rhs=Lt[:], start=False, stop=True))
                    t_omm = R.tok("tensor")
                    t_chunk_users[j].append(t_omm)

                    R.wait("tensor", t_ktm)
                    if ps_tokens["ds"]:
                        R.wait("tensor", ps_tokens["ds"])
                    R.add("tensor", lambda e, Sb=Sb: e.matmul(
                        out=ps_ds[:], lhsT=Ktm[:], rhs=Wt[:], start=True, stop=True))
                    t_dsmm = R.tok("tensor")
                    R.wait("vector", t_dsmm, t_sbf[h])
                    R.add("vector", lambda e, Sf=Sf: e.tensor_add(
                        out=Sf, in0=Sf, in1=ps_ds[:]))
                    t_sadd = ps_tokens["ds"] = R.tok("vector")
                    R.wait("scalar", t_sadd)
                    R.add("scalar", lambda e, Sf=Sf, Sb=Sb: e.copy(out=Sb, in_=Sf))
                    t_sbf[h] = R.tok("scalar")

                    R.wait("scalar", t_omm)
                    R.add("scalar", lambda e: e.square(out=sqo[:], in_=ps_o[:]))
                    t_sqo = R.tok("scalar")
                    R.wait("tensor", t_sqo)
                    if ps_tokens["s1"]:
                        R.wait("tensor", ps_tokens["s1"])
                    R.add("tensor", lambda e: e.matmul(
                        out=ps_s1[:], lhsT=ones_c[:], rhs=sqo[:], start=True, stop=True))
                    t_ss = R.tok("tensor")
                    R.wait("scalar", t_ss)
                    R.add("scalar", lambda e: e.activation(
                        out=rso[:], in_=ps_s1[:], func=AF.Ln, scale=1.0 / HD, bias=EPS_RMS))
                    ps_tokens["s1"] = R.tok("scalar")
                    R.add("scalar", lambda e: e.activation(
                        out=rso[:], in_=rso[:], func=AF.Exp, scale=-0.5))
                    t_rso = R.tok("scalar")
                    R.wait("tensor", t_rso)
                    if ps_tokens["b2"]:
                        R.wait("tensor", ps_tokens["b2"])
                    R.add("tensor", lambda e: e.matmul(
                        out=ps_b2[:], lhsT=ones_r[:], rhs=rso[:], start=True, stop=True))
                    t_b2 = R.tok("tensor")
                    R.add("scalar", lambda e, G_=G_: e.activation(
                        out=sgo[:], in_=G_, func=AF.Sigmoid))
                    t_sg = R.tok("scalar")
                    t_chunk_users[j].append(t_sg)
                    R.wait("vector", t_sg)
                    R.add("vector", lambda e: e.scalar_tensor_tensor(
                        out=tgo[:], in0=ps_o[:], scalar=nw_sb[:], in1=sgo[:],
                        op0=ALU.mult, op1=ALU.mult))
                    t_tgo = R.tok("vector")
                    ps_tokens["o"] = [t_sqo, t_tgo]
                    R.wait("vector", t_b2)
                    R.add("vector", lambda e, h=h, j=j: e.tensor_mul(
                        out=og_sb[:, h * T + j * C:h * T + (j + 1) * C],
                        in0=tgo[:], in1=ps_b2[:]))
                    t_og = ps_tokens["b2"] = R.tok("vector")
                    t_pe_head_prev = t_b2
                    t_tgo_prev = t_og

            t_og_last = R.tok("vector")
            R.wait("gpsimd", t_og_last)
            for h in range(HPC):
                R.add("gpsimd", lambda e, h=h: e.dma_start(
                    out=og_in.ap()[h * 128:(h + 1) * 128, :],
                    in_=og_sb[:, h * T:(h + 1) * T]), is_dma=True, slot="ogw")
            t_ogst = R.tok("gpsimd")
            R.wait("gpsimd", t_ogst)
            R.add("gpsimd", lambda e: e.collective_compute(
                "AllGather", ALU.bypass,
                replica_groups=[[0, 1, 2, 3, 4, 5, 6, 7]],
                ins=[og_in.ap()], outs=[og_out.ap()]))
            item, _, _ = R.last["gpsimd"]
            R.cnt["cc"] += 1
            item["incs"].append((R.sem["cc"], 1))
            t_cc = ("cc", R.cnt["cc"])
            item["tok0"] = t_cc

        if "og_in" in taps:
            R.wait("gpsimd", t_ogst)
            R.add("gpsimd", lambda e: e.dma_start(out=taps["og_in"].ap(),
                                                  in_=og_in.ap()), is_dma=True, slot="tap")
        phAB.close()
        R.barrier()

        phC = ExitStack()
        with phC:
            o_all = phC.enter_context(nc.sbuf_tensor("o_all", [128, DC * TQ], BF16))
            sq_c = phC.enter_context(nc.sbuf_tensor("sq_c", [128, TQ], BF16))
            row_a = phC.enter_context(nc.sbuf_tensor("row_a", [1, TQ], F32))
            row_b = phC.enter_context(nc.sbuf_tensor("row_b", [1, TQ], F32))
            row_bf = phC.enter_context(nc.sbuf_tensor("row_bf", [1, TQ], BF16))
            mb_sb = phC.enter_context(nc.sbuf_tensor("mb_sb", [128, TQ], F32))
            rb_sb = phC.enter_context(nc.sbuf_tensor("rb_sb", [128, TQ], F32))
            ps_r = [phC.enter_context(nc.psum_tensor(f"ps_r{i}", [1, 512], F32))
                    for i in range(4)]
            ps_bb = phC.enter_context(nc.psum_tensor("ps_bb", [128, 512], F32))
            phC1 = ExitStack()
            z_f = phC1.enter_context(nc.sbuf_tensor("z_f", [128, MT * TQ], F32))
            wot = [phC1.enter_context(nc.sbuf_tensor(f"wot{i}", [128, DC * 128], BF16))
                   for i in range(2)]
            gld = [phC1.enter_context(nc.sbuf_tensor(f"gld{i}", [128, TQ], BF16))
                   for i in range(4)]
            xs_t = [phC1.enter_context(nc.sbuf_tensor(f"xs_t{i}", [128, TQ], F32))
                    for i in range(2)]
            ps_at = [phC1.enter_context(nc.psum_tensor(f"ps_at{i}", [128, 512], F32))
                     for i in range(2)]

            R.wait("sync", t_cc)
            t_blend = []
            t_gld_user = {}
            for dc in range(DC):
                eng = "vector"
                t_prev = None
                for i8 in range(8):
                    gi = (dc * 8 + i8) % 4
                    if (dc * 8 + i8) >= 4:
                        R.wait("sync", t_gld_user[(dc * 8 + i8) - 4])
                    bq, kq = i8 // 4, i8 % 4
                    R.add("sync", lambda e, g=gld[gi], bq=bq, kq=kq, dc=dc: e.dma_start(
                        out=g[:],
                        in_=og_out.ap()[bq * 2048 + dc * 128:bq * 2048 + (dc + 1) * 128,
                                        kq * TQ:(kq + 1) * TQ]), is_dma=True, slot="gld%d" % gi)
                    t_g = R.tok("sync")
                    R.wait(eng, t_g)
                    dsta = o_all[:, dc * TQ:(dc + 1) * TQ]
                    scl = sel_sb[:, i8:i8 + 1]
                    if i8 == 0:
                        R.add(eng, lambda e, d=dsta, g=gld[gi], c=scl: e.tensor_scalar(
                            out=d, in0=g[:], scalar1=c, scalar2=None, op0=ALU.mult))
                    else:
                        R.add(eng, lambda e, d=dsta, g=gld[gi], c=scl:
                              e.scalar_tensor_tensor(out=d, in0=g[:], scalar=c,
                                                     in1=d, op0=ALU.mult, op1=ALU.add))
                    t_gld_user[dc * 8 + i8] = R.tok(eng)
                t_blend.append(t_gld_user[dc * 8 + 7])

            t_z_toks = {}
            t_zf_stats = []
            t_xs_user = {}
            t_at_free = [None, None]
            t_ssq_prev = [None, None]
            t_wot_user = {}
            for m in range(MT):
                if m >= 2:
                    R.wait("sync", *t_xs_user[m - 2])
                R.add("sync", lambda e, m=m: e.dma_start(
                    out=xs_t[m % 2][:], in_=xTs.ap()[m * 128:(m + 1) * 128, :]),
                    is_dma=True, slot="xs%d" % (m % 2))
                t_x = R.tok("sync")
                if m >= 2:
                    R.wait("sync", t_wot_user[m - 2])
                R.add("sync", lambda e, m=m: e.dma_start(
                    out=wot[m % 2][:],
                    in_=wo_bf.ap()[m].rearrange("p c j -> p (c j)")),
                    is_dma=True, slot="wot%d" % (m % 2))
                t_wo_ld = R.tok("sync")
                t_xs_user[m] = []
                for half in range(2):
                    pst = ps_at[half]
                    if m == 0 and half == 0:
                        R.wait("tensor", *t_blend)
                    R.wait("tensor", t_wo_ld)
                    if t_at_free[half]:
                        R.wait("tensor", t_at_free[half])
                    for kc in range(DC):
                        R.add("tensor", lambda e, pst=pst, kc=kc, m=m, half=half:
                              e.matmul(out=pst[:],
                                       lhsT=wot[m % 2][:, kc * 128:(kc + 1) * 128],
                                       rhs=o_all[:, kc * TQ + half * 512:kc * TQ + (half + 1) * 512],
                                       start=(kc == 0), stop=(kc == DC - 1)))
                    t_amm = R.tok("tensor")
                    t_wot_user[m] = t_amm
                    R.wait("vector", t_amm, t_x)
                    R.add("vector", lambda e, m=m, half=half, pst=pst: e.tensor_add(
                        out=z_f[:, m * TQ + half * 512:m * TQ + (half + 1) * 512],
                        in0=xs_t[m % 2][:, half * 512:(half + 1) * 512], in1=pst[:]))
                    t_z2 = R.tok("vector")
                    t_at_free[half] = t_z2
                    t_xs_user[m].append(t_z2)
                    t_z_toks[(m, half)] = t_z2
                    if "c1" in taps and m == 0 and half == 0:
                        R.wait("scalar", t_amm)
                        R.add("scalar", lambda e, pst=pst: e.copy(
                            out=sq_c[:, 0:512], in_=pst[:]))
                        t_tpc = R.tok("scalar")
                        R.wait("gpsimd", t_tpc, t_z2, t_x)
                        R.add("gpsimd", lambda e: e.dma_start(
                            out=taps["c1"].ap()[0][:, 0:512], in_=z_f[:, 0:512]),
                            is_dma=True, slot="tap")
                        R.add("gpsimd", lambda e: e.dma_start(
                            out=taps["c1"].ap()[1], in_=xs_t[0][:]),
                            is_dma=True, slot="tap")
                        R.add("gpsimd", lambda e: e.dma_start(
                            out=taps["c1b"].ap()[0][:, 0:512], in_=sq_c[:, 0:512]),
                            is_dma=True, slot="tap")
                        t_tp0 = R.tok("gpsimd")
                        R.wait("vector", t_tp0)
                        R.wait("scalar", t_tp0)
                        R.wait("tensor", t_tp0)
                    R.wait("tensor", t_z2)
                    R.add("tensor", lambda e, m=m, half=half: e.matmul(
                        out=ps_r[half][:], lhsT=ones_cf[:],
                        rhs=z_f[:, m * TQ + half * 512:m * TQ + (half + 1) * 512],
                        start=(m == 0), stop=(m == MT - 1)))
                    t_smm = R.tok("tensor")
                    R.wait("scalar", t_z2, t_ssq_prev[half])
                    R.add("scalar", lambda e, m=m, half=half: e.square(
                        out=sq_c[:, half * 512:(half + 1) * 512],
                        in_=z_f[:, m * TQ + half * 512:m * TQ + (half + 1) * 512]))
                    t_sq = R.tok("scalar")
                    R.wait("tensor", t_sq)
                    R.add("tensor", lambda e, m=m, half=half: e.matmul(
                        out=ps_r[2 + half][:], lhsT=ones_c[:],
                        rhs=sq_c[:, half * 512:(half + 1) * 512],
                        start=(m == 0), stop=(m == MT - 1)))
                    t_qmm = R.tok("tensor")
                    t_ssq_prev[half] = t_qmm
                    if m == MT - 1:
                        t_zf_stats = [t_smm, t_qmm]

            if "zfull" in taps:
                R.wait("gpsimd", *t_zf_stats)
                for mm_ in range(MT):
                    R.add("gpsimd", lambda e, mm_=mm_: e.dma_start(
                        out=taps["zfull"].ap()[mm_],
                        in_=z_f[:, mm_ * TQ:(mm_ + 1) * TQ]), is_dma=True, slot="tap")
                t_tpz = R.tok("gpsimd")
                R.wait("vector", t_tpz)
                R.wait("scalar", t_tpz)
                R.wait("tensor", t_tpz)

            def ln_rows(t_stats_done):
                toks = {}
                R.wait("scalar", *t_stats_done)
                for half in range(2):
                    R.add("scalar", lambda e, half=half: e.activation(
                        out=row_a[:, half * 512:(half + 1) * 512], in_=ps_r[half][:],
                        func=AF.Copy, scale=1.0 / D))
                    R.add("scalar", lambda e, half=half: e.square(
                        out=row_b[:, half * 512:(half + 1) * 512],
                        in_=row_a[:, half * 512:(half + 1) * 512]))
                    t_r = R.tok("scalar")
                    R.wait("vector", t_r)
                    R.add("vector", lambda e, half=half: e.scalar_tensor_tensor(
                        out=row_b[:, half * 512:(half + 1) * 512], in0=ps_r[2 + half][:],
                        scalar=1.0 / D, in1=row_b[:, half * 512:(half + 1) * 512],
                        op0=ALU.mult, op1=ALU.subtract))
                    t_v = R.tok("vector")
                    R.wait("scalar", t_v)
                    R.add("scalar", lambda e, half=half: e.activation(
                        out=row_b[:, half * 512:(half + 1) * 512],
                        in_=row_b[:, half * 512:(half + 1) * 512],
                        func=AF.Ln, bias=EPS_LN))
                    R.add("scalar", lambda e, half=half: e.activation(
                        out=row_b[:, half * 512:(half + 1) * 512],
                        in_=row_b[:, half * 512:(half + 1) * 512],
                        func=AF.Exp, scale=-0.5))
                    toks[half] = R.tok("scalar")
                R.wait("vector", toks[0], toks[1])
                R.add("vector", lambda e: e.tensor_copy(out=row_bf[:], in_=row_a[:]))
                t_mbf = R.tok("vector")
                t_bb_user = None
                for half in range(2):
                    R.wait("tensor", t_mbf)
                    if t_bb_user:
                        R.wait("tensor", t_bb_user)
                    R.add("tensor", lambda e, half=half: e.matmul(
                        out=ps_bb[:], lhsT=ones_r[:],
                        rhs=row_bf[0:1, half * 512:(half + 1) * 512],
                        start=True, stop=True))
                    t_bb = R.tok("tensor")
                    R.wait("vector", t_bb)
                    R.add("vector", lambda e, half=half: e.tensor_copy(
                        out=mb_sb[:, half * 512:(half + 1) * 512], in_=ps_bb[:]))
                    t_bb_user = R.tok("vector")
                R.wait("vector", t_bb_user)
                R.add("vector", lambda e: e.tensor_copy(out=row_bf[:], in_=row_b[:]))
                t_rbf = R.tok("vector")
                for half in range(2):
                    R.wait("tensor", t_rbf)
                    R.wait("tensor", t_bb_user)
                    R.add("tensor", lambda e, half=half: e.matmul(
                        out=ps_bb[:], lhsT=ones_r[:],
                        rhs=row_bf[0:1, half * 512:(half + 1) * 512],
                        start=True, stop=True))
                    t_bb = R.tok("tensor")
                    R.wait("vector", t_bb)
                    R.add("vector", lambda e, half=half: e.tensor_copy(
                        out=rb_sb[:, half * 512:(half + 1) * 512], in_=ps_bb[:]))
                    t_bb_user = R.tok("vector")
                return t_bb_user

            t_rows1 = ln_rows(t_zf_stats)

            x1_bf = o_all
            t_x1 = []
            for m in range(MT):
                if m == 0:
                    R.wait("vector", *t_zf_stats)
                R.add("vector", lambda e, m=m: e.tensor_sub(
                    out=z_f[:, m * TQ:(m + 1) * TQ],
                    in0=z_f[:, m * TQ:(m + 1) * TQ], in1=mb_sb[:]))
                R.add("vector", lambda e, m=m: e.tensor_mul(
                    out=x1_bf[:, m * TQ:(m + 1) * TQ],
                    in0=z_f[:, m * TQ:(m + 1) * TQ], in1=rb_sb[:]))
                t_x1.append(R.tok("vector"))

            t_rs_mm = None
            for m in range(MT):
                R.wait("tensor", t_x1[m])
                for half in range(2):
                    R.add("tensor", lambda e, m=m, half=half: e.matmul(
                        out=ps_r[half][:], lhsT=ones_c[:],
                        rhs=x1_bf[:, m * TQ + half * 512:m * TQ + (half + 1) * 512],
                        start=(m == 0), stop=(m == MT - 1)))
                    t_rs_mm = R.tok("tensor")
            R.wait("vector", t_rs_mm)
            R.add("vector", lambda e: e.tensor_copy(out=row_bf[:, 0:512], in_=ps_r[0][:]))
            R.add("vector", lambda e: e.tensor_copy(out=row_bf[:, 512:1024], in_=ps_r[1][:]))
            t_rs = R.tok("vector")
            if "c3" in taps:
                R.wait("gpsimd", t_rs, t_x1[0], t_x1[MT - 1])
                R.add("gpsimd", lambda e: e.dma_start(
                    out=taps["c3"].ap()[0], in_=x1_bf[:, 0:TQ]), is_dma=True, slot="tap")
                R.add("gpsimd", lambda e: e.dma_start(
                    out=taps["c3"].ap()[1], in_=x1_bf[:, 15 * TQ:16 * TQ]),
                    is_dma=True, slot="tap")
                R.add("gpsimd", lambda e: e.dma_start(
                    out=taps["c3"].ap()[2][0:1, :], in_=row_bf[:]), is_dma=True, slot="tap")
                t_tp3 = R.tok("gpsimd")
                R.wait("vector", t_tp3)
                R.wait("scalar", t_tp3)
                R.wait("tensor", t_tp3)
            phC1.close()

            phC2 = ExitStack()
            with phC2:
                hm1 = phC2.enter_context(nc.sbuf_tensor("hm1", [128, 31 * TQ], BF16))
                hm2 = phC2.enter_context(nc.sbuf_tensor("hm2", [128, 13 * TQ], BF16))
                ps_g = [phC2.enter_context(nc.psum_tensor(f"ps_g{i}", [128, 512], F32))
                        for i in range(2)]
                phC2w = ExitStack()
                guard_g = phC2w.enter_context(nc.sbuf_tensor("guard_g", [128, 1024], I32))
                wgi = phC2w.enter_context(nc.sbuf_tensor("wgi", [128, DC * 128], I32))
                wgb = [phC2w.enter_context(nc.sbuf_tensor(f"wgb{i}", [128, DC * 128], BF16))
                       for i in range(2)]
                wub = [phC2w.enter_context(nc.sbuf_tensor(f"wub{i}", [128, DC * 128], BF16))
                       for i in range(2)]
                gsil = phC2w.enter_context(nc.sbuf_tensor("gsil", [128, TQ], BF16))

                def hmv(h):
                    return hm1[:, h * TQ:(h + 1) * TQ] if h < 31 else \
                        hm2[:, (h - 31) * TQ:(h - 30) * TQ]

                t_wbuf_users = {}
                t_hm = {}
                t_hs_mm = None
                t_gu_free = {("g", 0): None, ("g", 1): None, ("u", 0): None, ("u", 1): None}
                t_cast_prev = None
                for h in range(HT):
                    par = h % 2
                    ce = cast3[h % 3]
                    t_cast = None
                    for (wsrc, dstb) in [(wqgT, wgb), (wquT, wub)]:
                        if t_cast_prev:
                            R.wait("sync", t_cast_prev)
                        R.add("sync", lambda e, w=wsrc, h=h: e.dma_start(
                            out=wgi.ap().rearrange("p (c j) -> p c j", j=128),
                            in_=w.ap()[h]), is_dma=True, slot="wgi")
                        t_lf = R.tok("sync")
                        R.wait(ce, t_lf)
                        if h >= 2:
                            R.wait(ce, *t_wbuf_users[h - 2])
                        copy_op(ce, dstb[par][:], wgi[:])
                        t_cast = t_cast_prev = R.tok(ce)
                    t_wbuf_users[h] = [t_cast]

                    R.wait("tensor", t_cast)
                    if h == 0:
                        R.wait("tensor", t_rs, *t_x1)
                    for half in range(2):
                        if t_gu_free[("g", half)]:
                            R.wait("tensor", t_gu_free[("g", half)])
                        for kc in range(DC):
                            R.add("tensor", lambda e, kc=kc, half=half, par=par: e.matmul(
                                out=ps_g[half][:],
                                lhsT=wgb[par][:, kc * 128:(kc + 1) * 128],
                                rhs=x1_bf[:, kc * TQ + half * 512:kc * TQ + (half + 1) * 512],
                                start=(kc == 0), stop=False))
                        R.add("tensor", lambda e, half=half: e.matmul(
                            out=ps_g[half][:], lhsT=neg128[:],
                            rhs=row_bf[0:1, half * 512:(half + 1) * 512],
                            start=False, stop=True))
                        t_gmm = R.tok("tensor")
                        R.wait("scalar", t_gmm)
                        if h > 0:
                            R.wait("scalar", t_hm.get((h - 1, half)))
                        R.add("scalar", lambda e, half=half, h=h: e.activation(
                            out=gsil[:, half * 512:(half + 1) * 512], in_=ps_g[half][:],
                            func=AF.Silu, scale=sg_sb[:, h:h + 1]))
                        t_gsil = R.tok("scalar")
                        R.wait("tensor", t_gsil)
                        for kc in range(DC):
                            R.add("tensor", lambda e, kc=kc, half=half, par=par: e.matmul(
                                out=ps_g[half][:],
                                lhsT=wub[par][:, kc * 128:(kc + 1) * 128],
                                rhs=x1_bf[:, kc * TQ + half * 512:kc * TQ + (half + 1) * 512],
                                start=(kc == 0), stop=False))
                        R.add("tensor", lambda e, half=half: e.matmul(
                            out=ps_g[half][:], lhsT=neg128[:],
                            rhs=row_bf[0:1, half * 512:(half + 1) * 512],
                            start=False, stop=True))
                        t_umm = R.tok("tensor")
                        t_wbuf_users[h].append(t_umm)
                        R.wait("vector", t_umm)
                        R.add("vector", lambda e, h=h, half=half: e.scalar_tensor_tensor(
                            out=hmv(h)[:, half * 512:(half + 1) * 512],
                            in0=ps_g[half][:], scalar=su_sb[:, h:h + 1],
                            in1=gsil[:, half * 512:(half + 1) * 512],
                            op0=ALU.mult, op1=ALU.mult))
                        t_hm[(h, half)] = R.tok("vector")
                        t_gu_free[("g", half)] = t_hm[(h, half)]
                        if "c3" in taps and half == 1 and h >= 40:
                            ti = h - 40
                            R.wait("gpsimd", t_hm[(h, half)])
                            R.add("gpsimd", lambda e, ti=ti: e.dma_start(
                                out=taps["c3"].ap()[ti], in_=hmv(40)),
                                is_dma=True, slot="tap")
                        R.wait("tensor", t_hm[(h, half)])
                        R.add("tensor", lambda e, h=h, half=half: e.matmul(
                            out=ps_r[2 + half][:], lhsT=ones_c[:],
                            rhs=hmv(h)[:, half * 512:(half + 1) * 512],
                            start=(h == 0), stop=(h == HT - 1)))
                        t_hs_mm = R.tok("tensor")

                R.wait("vector", t_hs_mm)
                R.add("vector", lambda e: e.tensor_copy(out=row_bf[:, 0:512],
                                                        in_=ps_r[2][:]))
                R.add("vector", lambda e: e.tensor_copy(out=row_bf[:, 512:1024],
                                                        in_=ps_r[3][:]))
                t_hs = R.tok("vector")
                if "c3" in taps:
                    R.wait("gpsimd", t_hs, t_hm[(0, 1)], t_hm[(43, 1)])
                    for ti, hh in enumerate([40, 41, 42, 43]):
                        buf = hmv(hh)
                        R.add("gpsimd", lambda e, ti=ti, buf=buf: e.dma_start(
                            out=taps["c3"].ap()[4 + ti], in_=buf), is_dma=True, slot="tap")
                    t_tp4 = R.tok("gpsimd")
                    R.wait("vector", t_tp4)
                    R.wait("scalar", t_tp4)
                    R.wait("tensor", t_tp4)
                phC2w.close()

                phC3w = ExitStack()
                guard_d = phC3w.enter_context(nc.sbuf_tensor("guard_d", [128, 1024], I32))
                wdi = phC3w.enter_context(nc.sbuf_tensor("wdi", [128, 22 * 128], I32))
                wdb = [phC3w.enter_context(nc.sbuf_tensor(f"wdb{i}", [128, HT * 128], BF16))
                       for i in range(2)]
                t_dbuf_users = {}
                t_dn_free = [None, None]
                t_z2_toks = []
                t_stats2 = []
                t_ssq_prev = [None, None]
                t_dcast_prev = None
                for m in range(MT):
                    par = m % 2
                    t_casts = []
                    for dh in range(2):
                        lo, hi = (0, 22) if dh == 0 else (22, HT)
                        if m == 0 and dh == 0:
                            R.wait("sync", t_hs)
                        if t_dcast_prev:
                            R.wait("sync", t_dcast_prev)
                        R.add("sync", lambda e, m=m, lo=lo, hi=hi: e.dma_start(
                            out=wdi.ap().rearrange("p (c j) -> p c j", j=128)
                            [:, 0:hi - lo],
                            in_=wqdT.ap()[m][:, lo:hi, :]), is_dma=True, slot="wdi")
                        t_lw = R.tok("sync")
                        ce = cast3[(2 * m + dh) % 3]
                        R.wait(ce, t_lw)
                        if m >= 2:
                            R.wait(ce, *t_dbuf_users[m - 2])
                        copy_op(ce, wdb[par][:, lo * 128:hi * 128],
                                wdi[:, 0:(hi - lo) * 128])
                        t_dcast_prev = R.tok(ce)
                        t_casts.append(t_dcast_prev)
                    t_dbuf_users[m] = list(t_casts)

                    R.wait("tensor", *t_casts)
                    if m == 0:
                        R.wait("tensor", t_hs, *[t_hm[(h, hf)] for h in range(HT)
                                                 for hf in range(2)])
                    for half in range(2):
                        if t_dn_free[half]:
                            R.wait("tensor", t_dn_free[half])
                        for h in range(HT):
                            R.add("tensor", lambda e, h=h, half=half, par=par: e.matmul(
                                out=ps_g[half][:],
                                lhsT=wdb[par][:, h * 128:(h + 1) * 128],
                                rhs=hmv(h)[:, half * 512:(half + 1) * 512],
                                start=(h == 0), stop=False))
                        R.add("tensor", lambda e, half=half: e.matmul(
                            out=ps_g[half][:], lhsT=neg128[:],
                            rhs=row_bf[0:1, half * 512:(half + 1) * 512],
                            start=False, stop=True))
                        t_dmm = R.tok("tensor")
                        t_dbuf_users[m].append(t_dmm)
                        R.wait("vector", t_dmm)
                        R.add("vector", lambda e, m=m, half=half: e.scalar_tensor_tensor(
                            out=x1_bf[:, m * TQ + half * 512:m * TQ + (half + 1) * 512],
                            in0=ps_g[half][:], scalar=sd_sb[:, m:m + 1],
                            in1=x1_bf[:, m * TQ + half * 512:m * TQ + (half + 1) * 512],
                            op0=ALU.mult, op1=ALU.add))
                        t_z2c = R.tok("vector")
                        t_dn_free[half] = t_z2c
                        t_z2_toks.append(t_z2c)
                        R.wait("tensor", t_z2c)
                        R.add("tensor", lambda e, m=m, half=half: e.matmul(
                            out=ps_r[half][:], lhsT=ones_c[:],
                            rhs=x1_bf[:, m * TQ + half * 512:m * TQ + (half + 1) * 512],
                            start=(m == 0), stop=(m == MT - 1)))
                        t_s1 = R.tok("tensor")
                        R.wait("scalar", t_z2c, t_ssq_prev[half])
                        R.add("scalar", lambda e, m=m, half=half: e.square(
                            out=sq_c[:, half * 512:(half + 1) * 512],
                            in_=x1_bf[:, m * TQ + half * 512:m * TQ + (half + 1) * 512]))
                        t_sq2 = R.tok("scalar")
                        R.wait("tensor", t_sq2)
                        R.add("tensor", lambda e, half=half, m=m: e.matmul(
                            out=ps_r[2 + half][:], lhsT=ones_c[:],
                            rhs=sq_c[:, half * 512:(half + 1) * 512],
                            start=(m == 0), stop=(m == MT - 1)))
                        t_s2 = R.tok("tensor")
                        t_ssq_prev[half] = t_s2
                        if m == MT - 1:
                            t_stats2 = [t_s1, t_s2]

                t_rows2 = ln_rows(t_stats2)
                phC3w.close()

                out_st = phC2.enter_context(nc.sbuf_tensor("out_st", [128, TQ], F32))
                t_out_dma = {}
                for m in range(MT):
                    R.wait("vector", t_rows2)
                    if m >= 1:
                        R.wait("vector", t_out_dma[m - 1])
                    R.add("vector", lambda e, m=m: e.tensor_sub(
                        out=out_st[:], in0=x1_bf[:, m * TQ:(m + 1) * TQ],
                        in1=mb_sb[:]))
                    R.add("vector", lambda e, m=m: e.tensor_mul(
                        out=out_st[:], in0=out_st[:], in1=rb_sb[:]))
                    t_o = R.tok("vector")
                    R.wait("gpsimd", t_o)
                    R.add("gpsimd", lambda e, m=m: e.dma_start(
                        out=outT.ap()[m * 128:(m + 1) * 128, :], in_=out_st[:]),
                        is_dma=True, slot="outw")
                    t_out_dma[m] = R.tok("gpsimd")
                final_wait(R, "gpsimd", t_out_dma[MT - 1])

        block = ctx.enter_context(nc.Block())
        R.replay(block)
    return nc


def _prep_inputs(inputs):
    x = np.ascontiguousarray(inputs["L0_attn_hidden_states"], dtype=np.float32)
    Wq = np.asarray(inputs["L0_attn_Wq"], np.float32)
    Wk = np.asarray(inputs["L0_attn_Wk"], np.float32)
    Wv = np.asarray(inputs["L0_attn_Wv"], np.float32)
    Wo = np.asarray(inputs["L0_attn_Wo"], np.float32)
    Wg = np.asarray(inputs["L0_attn_Wg"], np.float32)
    Wb = np.asarray(inputs["L0_attn_Wb"], np.float32)
    cwq = np.asarray(inputs["L0_attn_conv_wq"], np.float32)
    cwk = np.asarray(inputs["L0_attn_conv_wk"], np.float32)
    cwv = np.asarray(inputs["L0_attn_conv_wv"], np.float32)
    nw = np.asarray(inputs["L0_attn_norm_w"], np.float32)
    wqg = np.asarray(inputs["L0_ffn_wq_gate"], np.int32)
    wqu = np.asarray(inputs["L0_ffn_wq_up"], np.int32)
    wqd = np.asarray(inputs["L0_ffn_wq_down"], np.int32)
    sg = np.asarray(inputs["L0_ffn_s_gate"], np.float32)
    su = np.asarray(inputs["L0_ffn_s_up"], np.float32)
    sd = np.asarray(inputs["L0_ffn_s_down"], np.float32)

    xT3 = [np.ascontiguousarray(x[b].T.reshape(DC, 128, T).transpose(1, 0, 2))
           for b in range(B)]
    wqgT3 = np.ascontiguousarray(
        wqg.reshape(HT, 128, DC, 128).transpose(0, 3, 2, 1))
    wquT3 = np.ascontiguousarray(
        wqu.reshape(HT, 128, DC, 128).transpose(0, 3, 2, 1))
    wqdT3 = np.ascontiguousarray(
        wqd.reshape(MT, 128, HT, 128).transpose(0, 3, 2, 1))
    sg2 = np.ascontiguousarray(sg.reshape(HT, 128).T)
    su2 = np.ascontiguousarray(su.reshape(HT, 128).T)
    sd2 = np.ascontiguousarray(sd.reshape(MT, 128).T)
    nw2 = np.ascontiguousarray(nw.reshape(128, 1))
    wo3 = np.ascontiguousarray(Wo.reshape(DC, 128, D).transpose(1, 0, 2))

    def conv_layout(cwi):
        return np.ascontiguousarray(
            cwi.reshape(4, 128, 4).transpose(1, 0, 2).reshape(128, 16))

    def wslice(W, q):
        Ws = W[:, 512 * q:512 * (q + 1)]
        return Ws.reshape(DC, 128, 512).transpose(1, 0, 2)

    in_maps = []
    for c in range(8):
        b, q = c // 4, c % 4
        cols = slice(512 * q, 512 * (q + 1))
        wqkvg = np.ascontiguousarray(np.stack(
            [wslice(Wq, q), wslice(Wk, q), wslice(Wv, q), wslice(Wg, q)], axis=0))
        cwcat = np.concatenate([conv_layout(cwq[cols]), conv_layout(cwk[cols]),
                                conv_layout(cwv[cols])], axis=1)
        wb3 = np.ascontiguousarray(
            Wb[:, 4 * q:4 * (q + 1)].reshape(DC, 128, HPC).transpose(1, 0, 2))
        sel = np.zeros((128, 8), np.float32)
        sel[:, b * 4 + q] = 1.0
        in_maps.append({
            "xT": xT3[b],
            "xTs": np.ascontiguousarray(x[b].T[:, TQ * q:TQ * (q + 1)]),
            "wqkvg": wqkvg,
            "wb": wb3,
            "cw": cwcat,
            "normw": nw2,
            "wo": wo3,
            "wqgT": wqgT3,
            "wquT": wquT3,
            "wqdT": wqdT3,
            "sgv": sg2,
            "suv": su2,
            "sdv": sd2,
            "selmask": sel,
        })
    return in_maps


def kernel(**inputs):
    in_maps = _prep_inputs(inputs)
    nc = build_program()
    res = run_bass_kernel_spmd(nc, in_maps, core_ids=list(range(8)))
    out = np.empty((B, T, D), np.float32)
    for c in range(8):
        b, q = c // 4, c % 4
        out[b, TQ * q:TQ * (q + 1), :] = res.results[c]["outT"].T
    return out
